# version 14 (final)
# baseline (speedup 1.0000x reference)
import numpy as np
import ml_dtypes
import concourse.bass as bass
import concourse.mybir as mybir
from concourse.bass_utils import run_bass_kernel_spmd
from contextlib import ExitStack

F32 = mybir.dt.float32
BF16 = mybir.dt.bfloat16
I32 = mybir.dt.int32
U8 = mybir.dt.uint8
AF = mybir.ActivationFunctionType
AX = mybir.AxisListType
ALU = mybir.AluOpType
bfdt = ml_dtypes.bfloat16

DEBUG = False

D = 2048
S = 2048
H = 16
DH = 128
II = 8192
NT = S // 128
NDC = D // 128
NIT = II // 128
NQ = 4
TS = 512
NTS = S // TS
ISLAB = 256
NSLAB = II // ISLAB
INV_SQRT_DH = 0.08838834764831845
NEG_BIG = -1.0e9
EPS = 1e-5


class Stream:
    def __init__(self, name, unit):
        self.name = name
        self.unit = unit
        self.sem = None
        self.cnt = 0
        self.ops = []
        self.marks = {}
        self.covered = True

    def op(self, fn):
        self.ops.append(("o", fn))
        self.covered = False

    def opi(self, fn):
        self.cnt += self.unit
        self.ops.append(("i", fn))
        self.covered = True
        return (self, self.cnt)

    def dma(self, slot, fn, waits=()):
        for t in waits:
            self.wait(t)
        slot.cnt += 16
        self.ops.append(("d", fn, slot))
        self.covered = True
        return (slot, slot.cnt)

    def wait(self, tok):
        if tok is None:
            return
        src, val = tok
        if src is self and self.cnt < val:
            raise RuntimeError(f"{self.name}: wait on own future {val}>{self.cnt}")
        if self.marks.get(src.name, 0) >= val:
            return
        self.marks[src.name] = val
        self.ops.append(("w", src, val))

    def waits(self, toks):
        for t in toks:
            self.wait(t)


class Slot:
    def __init__(self, name):
        self.name = name
        self.sem = None
        self.cnt = 0


def build(debug=False, ffn_u8=True):
    nc = bass.Bass("TRN2", target_bir_lowering=False)

    x_h = nc.dram_tensor("x", [S, D], F32, kind="ExternalInput").ap()
    wq_h = nc.dram_tensor("wqH", [H, 128, NDC, 128], F32, kind="ExternalInput").ap()
    wk_h = nc.dram_tensor("wkH", [H, 128, NDC, 128], F32, kind="ExternalInput").ap()
    wv_h = nc.dram_tensor("wvH", [H, 128, NDC, 128], F32, kind="ExternalInput").ap()
    wo_h = nc.dram_tensor("woH", [128, NDC, D], F32, kind="ExternalInput").ap()
    WQDT = U8 if ffn_u8 else I32
    wg_h = nc.dram_tensor("wgH", [NSLAB, 128, NDC, ISLAB], WQDT, kind="ExternalInput").ap()
    wu_h = nc.dram_tensor("wuH", [NSLAB, 128, NDC, ISLAB], WQDT, kind="ExternalInput").ap()
    wd_h = nc.dram_tensor("wdH", [NDC, 128, NIT, 128], WQDT, kind="ExternalInput").ap()
    gsc_h = nc.dram_tensor("gscH", [128, NIT], F32, kind="ExternalInput").ap()
    usc_h = nc.dram_tensor("uscH", [128, NIT], F32, kind="ExternalInput").ap()
    dsc_h = nc.dram_tensor("dscH", [128, NDC], F32, kind="ExternalInput").ap()
    id16_h = nc.dram_tensor("id16H", [128, 128], BF16, kind="ExternalInput").ap()
    id32_h = nc.dram_tensor("id32H", [128, 128], F32, kind="ExternalInput").ap()
    cmask_h = nc.dram_tensor("cmaskH", [128, 128], F32, kind="ExternalInput").ap()
    out_h = nc.dram_tensor("out", [S, D], F32, kind="ExternalOutput").ap()
    if debug:
        x1s = nc.dram_tensor("x1s", [S, D], F32, kind="ExternalOutput").ap()
        zs = nc.dram_tensor("zs", [NT, 128, H, 128], BF16, kind="ExternalOutput").ap()
    else:
        x1s = nc.dram_tensor("x1s", [S, D], F32).ap()
        zs = nc.dram_tensor("zs", [NT, 128, H, 128], BF16).ap()

    def tapdecl(name, shape, dtype):
        if debug:
            return nc.dram_tensor(name, shape, dtype, kind="ExternalOutput").ap()
        return None

    t_ln1 = tapdecl("t_ln1", [128, NDC * S], BF16)
    t_kf = tapdecl("t_kf", [128, S], BF16)
    t_qf = tapdecl("t_qf", [128, S], BF16)
    t_vt = tapdecl("t_vt", [128, S], BF16)
    t_p = tapdecl("t_p", [128, S], BF16)
    t_pa = tapdecl("t_pa", [128, S], BF16)
    t_ln2 = tapdecl("t_ln2", [128, NDC * TS], BF16)
    t_h = tapdecl("t_h", [128, NIT * TS], BF16)
    t_y = tapdecl("t_y", [128, D], F32)

    PE = Stream("pe", 1)
    VE = Stream("ve", 1)
    AE = Stream("ae", 1)
    GD = Stream("gq", 16)
    SD = Stream("sq", 16)
    ALLS = (PE, VE, AE, GD, SD)
    slots = {n: Slot(n) for n in
             ["wg0", "wg1", "wu0", "wu1", "wqkv", "x0", "x1", "z0", "z1",
              "swr", "gwr", "stg", "wx0", "wx1"]}

    es = ExitStack()
    sb = lambda name, shape, dt: es.enter_context(nc.sbuf_tensor(name, shape, dt))
    psal = lambda name, shape, dt: es.enter_context(nc.psum_tensor(name, shape, dt))

    id16 = sb("id16", [128, 128], BF16)
    id32 = sb("id32", [128, 128], F32)
    cmask = sb("cmask", [128, 128], F32)
    ones16 = sb("ones16", [128, 128], BF16)
    ones32r = sb("ones32r", [1, 128], F32)
    epsv = sb("epsv", [128, 1], F32)
    gsc = sb("gsc", [128, NIT], F32)
    usc = sb("usc", [128, NIT], F32)
    dsc = sb("dsc", [128, NDC], F32)
    arA = sb("arA", [128, 32768], BF16)
    arB = sb("arB", [128, 16384], BF16)
    arC = sb("arC", [128, 22528], BF16)
    arD = sb("arD", [128, 6144], F32)
    arE = sb("arE", [128, 9216], F32)
    bns = [sb(f"bns{i}", [128, 24], F32) for i in range(2)]
    mv = [sb(f"mv{i}", [128, 2], F32) for i in range(2)]
    sdv = [sb(f"sd{i}", [128, 1], F32) for i in range(2)]
    rstd = [sb(f"rstd{i}", [128, 1], F32) for i in range(2)]
    negmu = [sb(f"negmu{i}", [128, 1], F32) for i in range(2)]
    nbv = [sb(f"nb{i}", [128, 1], F32) for i in range(2)]
    sev = [sb(f"se{i}", [128, 1], F32) for i in range(2)]
    rcv = [sb(f"rc{i}", [128, 1], F32) for i in range(2)]
    mxv = [sb(f"mx{i}", [128, 1], F32) for i in range(2)]
    negln = sb("negln", [1, TS], BF16)
    stgscr = sb("stgscr", [1, 4], F32)
    neghs = sb("neghs", [1, TS], F32)

    def lntm(p):
        return arC[:, 18432 + p * 2048: 18432 + (p + 1) * 2048]

    PS_BIG = psal("psbig", [128, 2048], F32)
    PS_A = psal("psa", [128, 512], F32)
    PS_Z = psal("psz", [128, 512], F32)
    PS_T = psal("pst", [128, 2048], BF16)

    for st in ALLS:
        st.sem = es.enter_context(nc.semaphore(f"sem_{st.name}"))
    for n, sl in slots.items():
        sl.sem = es.enter_context(nc.semaphore(f"sem_{n}"))

    bank_war = {"A": None, "Z": None}

    def barrier():
        for st in ALLS:
            if not st.covered and st.ops:
                raise RuntimeError(f"barrier: {st.name} tail uncovered")
        toks = [(st, st.cnt) for st in ALLS] + \
               [(sl, sl.cnt) for sl in slots.values()]
        for st in ALLS:
            for t in toks:
                if t[0] is not st:
                    st.wait(t)

    GD.dma(slots["wqkv"], lambda e: e.dma_start(out=id16[:], in_=id16_h))
    GD.dma(slots["wqkv"], lambda e: e.dma_start(out=id32[:], in_=id32_h))
    GD.dma(slots["wqkv"], lambda e: e.dma_start(out=cmask[:], in_=cmask_h))
    GD.dma(slots["wqkv"], lambda e: e.dma_start(out=gsc[:], in_=gsc_h))
    GD.dma(slots["wqkv"], lambda e: e.dma_start(out=usc[:], in_=usc_h))
    c_last = GD.dma(slots["wqkv"], lambda e: e.dma_start(out=dsc[:], in_=dsc_h))
    t_ones = VE.opi(lambda e: e.memset(ones16[:], 1.0))
    t_ones32 = VE.opi(lambda e: e.memset(ones32r[:], 1.0))
    t_eps = VE.opi(lambda e: e.memset(epsv[:], EPS))

    def emit_ln(src_ap, par, pre_toks=()):
        b, m = bns[par], mv[par]
        VE.waits(pre_toks)
        VE.wait(t_eps)
        tb = None
        for c in range(4):
            tb = VE.opi(lambda e, b=b, c=c: e.bn_stats(
                b[:, 6 * c:6 * (c + 1)], src_ap[:, 512 * c:512 * (c + 1)]))
        VE.wait(tb)
        ta = VE.opi(lambda e, b=b, m=m: e.bn_aggr(m[:], b[:]))
        AE.wait(ta)
        ts_ = AE.opi(lambda e, m=m, par=par: e.activation(
            sdv[par][:], m[:, 1:2], AF.Sqrt, bias=epsv[:], scale=1.0))
        VE.wait(ts_)
        tr = VE.opi(lambda e, par=par: e.reciprocal(rstd[par][:], sdv[par][:]))
        tn = VE.opi(lambda e, m=m, par=par: e.tensor_scalar_mul(
            negmu[par][:], m[:, 0:1], -1.0))
        return negmu[par], rstd[par], (tr, tn)

    tr_half = [0]
    tr_evac = [None, None]
    tr_alt = [0]
    tr_cover = {"ve": None, "ae": None}

    def transpose4(srcs, dst3, src_tok=None, dst_war=(), f32=False, alt=False):
        n = len(srcs)
        half = tr_half[0]
        tr_half[0] ^= 1
        if f32:
            base = 512 + half * 512 if False else half * 512
            pst, ident = PS_BIG, id32
        else:
            base = half * 1024
            pst, ident = PS_T, id16
        if src_tok is not None:
            PE.wait(src_tok)
        if tr_evac[half] is not None:
            PE.wait(tr_evac[half])
        tt = None
        for i, s_ap in enumerate(srcs):
            tt = PE.opi(lambda e, s_ap=s_ap, i=i, pst=pst, base=base, ident=ident:
                        e.transpose(pst[:, base + i * 128: base + (i + 1) * 128],
                                    s_ap, ident[:]))
        if alt:
            tr_alt[0] ^= 1
            ev = VE if tr_alt[0] else AE
        else:
            ev = AE
        ev.wait(tt)
        ev.waits(dst_war)
        src3 = pst[:, base: base + n * 128].rearrange("p (n r) -> p n r", n=n, r=128)
        if ev is AE:
            tok = ev.opi(lambda e, dst3=dst3, src3=src3: e.copy(dst3, src3))
            tr_cover["ae"] = tok
        else:
            tok = ev.opi(lambda e, dst3=dst3, src3=src3: e.tensor_copy(dst3, src3))
            tr_cover["ve"] = tok
        tr_evac[half] = tok
        return tok


    ln1_fm = arA
    ln1_3d = ln1_fm.rearrange("p (dc r) -> p dc r", dc=NDC, r=S)
    wq_sb, wk_sb, wv_sb = arB[:, 0:2048], arB[:, 2048:4096], arB[:, 4096:6144]
    vtmp = [arB[:, 6144:6656], arB[:, 6656:7168]]
    z_sb = arB[:, 7168:9216]
    arEb = arE[:, :].bitcast(BF16)
    kset = [(arC[:, 0:2048], arC[:, 2048:4096], arC[:, 4096:6144]),
            (arEb[:, 0:2048], arEb[:, 2048:4096], arEb[:, 4096:6144])]
    p_sb = [arC[:, 6144:8192], arC[:, 8192:10240],
            arEb[:, 6144:8192], arEb[:, 8192:10240]]
    pT = arC[:, 10240:18432]
    pT_3d = pT.rearrange("p (kc r) -> p kc r", kc=16, r=512)
    SC = [PS_BIG[:, 0:1024], PS_BIG[:, 1024:2048]]
    projb = {"A": PS_A, "Z": PS_Z}
    sm2 = [sb(f"sm2{i}", [128, 16], F32) for i in range(4)]
    MX0, MX1, M01, NB0, NB1, SE0, SE1, D0, E0, S0E0, SSUM, RC, F0 = range(13)

    wqkv_read = [None]
    z_store_tok = None
    region_free = [None, None]
    pv_prev = None
    zevac_prev = None
    p_war = [None] * 4
    chain_war = [None] * 4
    proj_final = [None, None]

    def load_head_weights(h, war):
        GD.dma(slots["wqkv"], lambda e, h=h: e.dma_start(
            out=wq_sb.rearrange("p (dc e) -> p dc e", e=128), in_=wq_h[h]),
            waits=war)
        GD.dma(slots["wqkv"], lambda e, h=h: e.dma_start(
            out=wk_sb.rearrange("p (dc e) -> p dc e", e=128), in_=wk_h[h]))
        return GD.dma(slots["wqkv"], lambda e, h=h: e.dma_start(
            out=wv_sb.rearrange("p (dc e) -> p dc e", e=128), in_=wv_h[h]))

    wload = [load_head_weights(0, (c_last,))]

    def make_proj_chunks(h):
        par = h % 2
        kf_, qf_, vt_ = kset[par]
        vt3 = vt_.rearrange("p (kc r) -> p kc r", kc=16, r=128)
        chunks = []

        def mk_half(wsb, tc, half):
            def em():
                PE.wait(wload[0])
                if half == 0:
                    PE.wait(bank_war["A"])
                for dc in range(8 * half, 8 * half + 8):
                    tmm = PE.opi(lambda e, wsb=wsb, dc=dc, tc=tc: e.matmul(
                        PS_A[:, :], wsb[:, dc * 128:(dc + 1) * 128],
                        ln1_fm[:, dc * 2048 + tc * 512: dc * 2048 + (tc + 1) * 512],
                        start=(dc == 0), stop=(dc == NDC - 1)))
                return tmm
            return em

        def mk_kq_b(wsb, dstfm, tc, scale):
            emh = mk_half(wsb, tc, 1)

            def em():
                tmm = emh()
                wqkv_read[0] = tmm
                AE.wait(tmm)
                if scale is None:
                    tev = AE.opi(lambda e, dstfm=dstfm, tc=tc: e.copy(
                        dstfm[:, tc * 512:(tc + 1) * 512], PS_A[:, :]))
                else:
                    tev = AE.opi(lambda e, dstfm=dstfm, tc=tc, scale=scale: e.mul(
                        dstfm[:, tc * 512:(tc + 1) * 512], PS_A[:, :], scale))
                bank_war["A"] = tev
                proj_final[par] = tev
            return em

        def mk_v_b(tc):
            emh = mk_half(wv_sb, tc, 1)

            def em():
                tmm = emh()
                wqkv_read[0] = tmm
                AE.wait(tmm)
                tev = AE.opi(lambda e, tc=tc: e.copy(
                    vtmp[tc % 2][:, :], PS_A[:, :]))
                bank_war["A"] = tev
                srcs = [vtmp[tc % 2][:, i * 128:(i + 1) * 128] for i in range(4)]
                tvt = transpose4(srcs, vt3[:, 4 * tc:4 * tc + 4, :], src_tok=tev,
                                 alt=True)
                bank_war["A"] = tvt
                proj_final[par] = tvt
            return em

        for tc in range(4):
            chunks.append(mk_half(wk_sb, tc, 0))
            chunks.append(mk_kq_b(wk_sb, kf_, tc, None))
        for tc in range(4):
            chunks.append(mk_half(wq_sb, tc, 0))
            chunks.append(mk_kq_b(wq_sb, qf_, tc, INV_SQRT_DH))
        for tc in range(4):
            chunks.append(mk_v(tc)) if False else None
            chunks.append(mk_half(wv_sb, tc, 0))
            chunks.append(mk_v_b(tc))
        return chunks


    h0_chunks = make_proj_chunks(0)
    for _st in (PE, VE, AE):
        _st.wait((slots["wqkv"], slots["wqkv"].cnt))
    xt = [arD[:, 0:2048], arD[:, 2048:4096]]
    xsl = [slots["x0"], slots["x1"]]
    xt_read = [None, None]
    lntm_read = [None, None]
    last_ln1_tok = None
    for t in range(NT):
        par = t % 2
        war = (xt_read[par],) if xt_read[par] else ()
        tx = SD.dma(xsl[par], lambda e, t=t, par=par: e.dma_start(
            out=xt[par][:, :], in_=x_h[t * 128:(t + 1) * 128, :]), waits=war)
        nmu, rsd, rdy = emit_ln(xt[par], par, pre_toks=(tx,))
        VE.waits(rdy)
        VE.wait(lntm_read[par])
        ltm = lntm(par)
        tnorm = VE.opi(lambda e, par=par, nmu=nmu, rsd=rsd: e.tensor_scalar(
            lntm(par)[:, :], xt[par][:, :], nmu[:], rsd[:], ALU.add, ALU.mult))
        xt_read[par] = tnorm
        for g in range(2):
            srcs = [ltm[:, (8 * g + i) * 128:(8 * g + i + 1) * 128] for i in range(8)]
            dst3 = ln1_3d[:, 8 * g:8 * g + 8, t * 128:(t + 1) * 128]
            last_ln1_tok = transpose4(srcs, dst3, src_tok=tnorm)
        lntm_read[par] = last_ln1_tok
        if t % 4 == 3:
            tc = t // 4
            PE.wait(last_ln1_tok)
            for ci in (2 * tc, 2 * tc + 1, 8 + 2 * tc, 8 + 2 * tc + 1,
                       16 + 2 * tc, 16 + 2 * tc + 1):
                h0_chunks[ci]()
    if debug:
        SD.wait(last_ln1_tok)
        SD.dma(slots["swr"], lambda e: e.dma_start(out=t_ln1, in_=ln1_fm[:, :]))

    barrier()

    wload[0] = load_head_weights(1, (wqkv_read[0],))

    pass_idx = [0]

    woT_3dE = arA.rearrange("p (fc o) -> p fc o", fc=H, o=D)
    wo_toks = []
    for h in range(H):
        par_h = h % 2
        kf_, qf_, vt_ = kset[par_h]
        PE.wait(proj_final[par_h])
        PE.wait(tr_cover["ve"])
        PE.wait(tr_cover["ae"])
        if h == H - 1:
            for g in range(4):
                wo_toks.append(GD.dma(slots["wqkv"], lambda e, g=g: e.dma_start(
                    out=woT_3dE[:, 4 * g:4 * (g + 1), :],
                    in_=wo_h[:, 4 * g:4 * (g + 1), :]),
                    waits=(wqkv_read[0],) if g == 0 else ()))
        fillers = make_proj_chunks(h + 1) if h + 1 < H else []
        fill_i = 0
        pending_tr = []
        half_pend = [None] * 4
        tms_cur = [None]

        def emit_transposes(qt, ctok, h=h, par_h=par_h):
            parq = qt % 4
            if qt % 4 == 0:
                Q = qt // 4
                VE.wait(pv_prev)
                tms = None
                for kc in range(4 * Q, 4 * Q + 4):
                    tms = VE.opi(lambda e, kc=kc: e.memset(
                        pT[:, kc * 512:(kc + 1) * 512], 0.0))
                tms_cur[0] = tms
            last_tr = None
            j = qt % 4
            for g in range((qt + 8) // 8):
                k0 = 8 * g
                nk = min(8, qt + 1 - k0)
                srcs = [p_sb[parq][:, (k0 + i) * 128:(k0 + i + 1) * 128]
                        for i in range(nk)]
                dst3 = pT_3d[:, k0:k0 + nk, j * 128:(j + 1) * 128]
                last_tr = transpose4(srcs, dst3, src_tok=ctok,
                                     dst_war=(tms_cur[0],), alt=True)
            p_war[parq] = last_tr
            return last_tr

        def emit_pv(Q, h=h, vt_=vt_):
            nonlocal pv_prev, zevac_prev, z_store_tok
            nkc = 4 * (Q + 1)
            PE.wait(p_war[0])
            PE.wait(p_war[1])
            PE.wait(tr_cover["ve"])
            PE.wait(tr_cover["ae"])
            PE.wait(bank_war["Z"])
            tpv = None
            for kc in range(nkc):
                tpv = PE.opi(lambda e, kc=kc, nkc=nkc: e.matmul(
                    PS_Z[:, :], vt_[:, kc * 128:(kc + 1) * 128],
                    pT[:, kc * 512:(kc + 1) * 512],
                    start=(kc == 0), stop=(kc == nkc - 1)))
            pv_prev = tpv
            AE.wait(tpv)
            AE.wait(z_store_tok)
            zevac_prev = AE.opi(lambda e, Q=Q: e.copy(
                z_sb[:, Q * 512:(Q + 1) * 512], PS_Z[:, :]))
            bank_war["Z"] = zevac_prev

        passes = []
        for qt in range(16):
            W = (qt + 1) * 128
            if W <= 1024:
                passes.append((qt, 0, W))
            else:
                passes.append((qt, 0, 1024))
                passes.append((qt, 1024, W))

        for (qt, lo, hi) in passes:
            parq = qt % 4
            smp = sm2[parq]
            mx0, mx1, m01 = smp[:, MX0:MX0+1], smp[:, MX1:MX1+1], smp[:, M01:M01+1]
            nb0, nb1 = smp[:, NB0:NB0+1], smp[:, NB1:NB1+1]
            se0, se1 = smp[:, SE0:SE0+1], smp[:, SE1:SE1+1]
            d0, e0 = smp[:, D0:D0+1], smp[:, E0:E0+1]
            s0e0, ssum = smp[:, S0E0:S0E0+1], smp[:, SSUM:SSUM+1]
            rc, f0 = smp[:, RC:RC+1], smp[:, F0:F0+1]
            pqt = p_sb[parq]
            W = (qt + 1) * 128
            w = hi - lo
            R = pass_idx[0] % 2
            pass_idx[0] += 1
            reg = SC[R]
            PE.wait(region_free[R])
            tsc = None
            for c in range((w + 511) // 512):
                w_c = min(512, w - c * 512)
                tsc = PE.opi(lambda e, c=c, w_c=w_c, qt=qt, lo=lo, reg=reg,
                             qf_=qf_, kf_=kf_: e.matmul(
                    reg[:, c * 512: c * 512 + w_c],
                    qf_[:, qt * 128:(qt + 1) * 128],
                    kf_[:, lo + c * 512: lo + c * 512 + w_c],
                    start=True, stop=True))
            first = (lo == 0)
            last = (hi == W)
            if first and not last:
                VE.wait(tsc)
                VE.wait(chain_war[parq])
                tmx0 = VE.opi(lambda e, reg=reg, mx0=mx0: e.reduce_max(
                    mx0, reg[:, 0:1024], axis=AX.X, negate=True))
                half_pend[parq] = (R, tmx0)
                continue_chain = False
            else:
                continue_chain = True
            if continue_chain:
                VE.wait(tsc)
                if first:
                    VE.wait(chain_war[parq])
                tmk = VE.opi(lambda e, reg=reg, W=W, lo=lo: e.tensor_tensor(
                    reg[:, W - 128 - lo:W - lo], reg[:, W - 128 - lo:W - lo],
                    cmask[:], ALU.add))
                VE.wait(tmk)
                if first and last:
                    tmx = VE.opi(lambda e, reg=reg, w=w, mx0=mx0: e.reduce_max(
                        mx0, reg[:, 0:w], axis=AX.X, negate=True))
                    AE.wait(tmx)
                    AE.wait(p_war[parq])
                    AE.wait(tr_cover["ve"])
                    AE.wait(tr_cover["ae"])
                    texp = AE.opi(lambda e, reg=reg, W=W, pqt=pqt, mx0=mx0,
                                  se0=se0: e.activation(
                        pqt[:, 0:W], reg[:, 0:W], AF.Exp,
                        bias=mx0, scale=1.0, accum_out=se0))
                    region_free[R] = texp
                    VE.wait(texp)
                    trc = VE.opi(lambda e, rc=rc, se0=se0: e.reciprocal(rc, se0))
                    AE.wait(trc)
                    ctok = AE.opi(lambda e, W=W, pqt=pqt, rc=rc: e.mul(
                        pqt[:, 0:W], pqt[:, 0:W], rc))
                    chain_war[parq] = ctok
                    pending_tr.append((qt, ctok))
                else:
                    Ra, tmx0 = half_pend[parq]
                    rega = SC[Ra]
                    tmx1 = VE.opi(lambda e, reg=reg, w=w, mx1=mx1: e.reduce_max(
                        mx1, reg[:, 0:w], axis=AX.X, negate=True))
                    VE.wait(tmx1)
                    tm01 = VE.opi(lambda e, nb1=nb1, mx0=mx0, mx1=mx1:
                                  e.tensor_tensor(nb1, mx0, mx1, ALU.min))
                    AE.wait(tm01)
                    AE.wait(p_war[parq])
                    AE.wait(tr_cover["ve"])
                    AE.wait(tr_cover["ae"])
                    texpa = AE.opi(lambda e, rega=rega, pqt=pqt, nb1=nb1,
                                   se0=se0: e.activation(
                        pqt[:, 0:1024], rega[:, 0:1024], AF.Exp,
                        bias=nb1, scale=1.0, accum_out=se0))
                    region_free[Ra] = texpa
                    texpb = AE.opi(lambda e, reg=reg, w=w, W=W, pqt=pqt,
                                   nb1=nb1, se1=se1: e.activation(
                        pqt[:, 1024:W], reg[:, 0:w], AF.Exp,
                        bias=nb1, scale=1.0, accum_out=se1))
                    region_free[R] = texpb
                    VE.wait(texpb)
                    tsum = VE.opi(lambda e, ssum=ssum, se0=se0, se1=se1:
                                  e.tensor_add(ssum, se0, se1))
                    VE.wait(tsum)
                    trc = VE.opi(lambda e, rc=rc, ssum=ssum: e.reciprocal(
                        rc, ssum))
                    AE.wait(trc)
                    ctok = AE.opi(lambda e, pqt=pqt, W=W, rc=rc: e.mul(
                        pqt[:, 0:W], pqt[:, 0:W], rc))
                    chain_war[parq] = ctok
                    pending_tr.append((qt, ctok))
            if debug and h == 0 and qt == 10 and hi == W:
                SD.wait(chain_war[2])
                ttap = SD.dma(slots["swr"], lambda e: e.dma_start(
                    out=t_pa, in_=p_sb[2][:, :]))
                p_war[2] = ttap
            if fill_i < len(fillers):
                fillers[fill_i]()
                fill_i += 1
            while len(pending_tr) > 0:
                qt2, ct2 = pending_tr.pop(0)
                emit_transposes(qt2, ct2)
                if qt2 % 4 == 3:
                    emit_pv(qt2 // 4)
        while pending_tr:
            qt2, ct2 = pending_tr.pop(0)
            emit_transposes(qt2, ct2)
            if qt2 % 4 == 3:
                emit_pv(qt2 // 4)
        while fill_i < len(fillers):
            fillers[fill_i]()
            fill_i += 1
        if h + 2 < H:
            wload[0] = load_head_weights(h + 2, (wqkv_read[0],))
        SD.wait(zevac_prev)
        z_store_tok = SD.dma(slots["swr"], lambda e, h=h: e.dma_start(
            out=zs[:, :, h, :].rearrange("t p c -> p t c"),
            in_=z_sb.rearrange("p (t c) -> p t c", c=128)))
        if debug and h == 0:
            SD.wait(proj_final[1] if proj_final[1] else z_store_tok)
            SD.dma(slots["swr"], lambda e: e.dma_start(out=t_kf, in_=kset[0][0][:, :]))
            SD.dma(slots["swr"], lambda e: e.dma_start(out=t_qf, in_=kset[0][1][:, :]))
            SD.dma(slots["swr"], lambda e: e.dma_start(out=t_vt, in_=kset[0][2][:, :]))
            SD.wait(chain_war[3])
            SD.dma(slots["swr"], lambda e: e.dma_start(out=t_p, in_=p_sb[3][:, :]))

    barrier()

    ln2_ready0 = [None]
    stg0_bufs = [arE[:, 4096:6144], arE[:, 6144:8192]]
    x1a = [arD[:, 0:2048], arD[:, 2048:4096]]
    ln2_fm = arC[:, 0:8192]
    ln2_3d = ln2_fm.rearrange("p (dc r) -> p dc r", dc=NDC, r=512)
    ln2_read = None

    def emit_ln2_stage(T, bufs=None, lslots=None, store_toks=None):
        nonlocal ln2_read
        bufs = bufs if bufs is not None else x1a
        lslots = lslots if lslots is not None else xsl
        ln2_ready = None
        for tt in range(4):
            t_abs = 4 * T + tt
            par = tt % 2
            war = []
            if bufs is x1a:
                war.append((slots["gwr"], slots["gwr"].cnt))
                if xt_read[par]:
                    war.append(xt_read[par])
            if store_toks is not None:
                war.append(store_toks[t_abs])
            tx = SD.dma(lslots[par], lambda e, t_abs=t_abs, par=par, bufs=bufs:
                        e.dma_start(
                out=bufs[par][:, :], in_=x1s[t_abs * 128:(t_abs + 1) * 128, :]),
                waits=war)
            nmu, rsd, rdy = emit_ln(bufs[par], par, pre_toks=(tx,))
            VE.waits(rdy)
            VE.wait(lntm_read[par])
            ltm = lntm(par)
            tnorm = VE.opi(lambda e, par=par, nmu=nmu, rsd=rsd, bufs=bufs:
                           e.tensor_scalar(
                lntm(par)[:, :], bufs[par][:, :], nmu[:], rsd[:],
                ALU.add, ALU.mult))
            if bufs is x1a:
                xt_read[par] = tnorm
            for g in range(2):
                srcs = [ltm[:, (8 * g + i) * 128:(8 * g + i + 1) * 128]
                        for i in range(8)]
                dst3 = ln2_3d[:, 8 * g:8 * g + 8, tt * 128:(tt + 1) * 128]
                ln2_ready = transpose4(srcs, dst3, src_tok=tnorm,
                                       dst_war=(ln2_read,) if ln2_read else ())
            lntm_read[par] = ln2_ready
        return ln2_ready

    woT = arA
    woT_3d = woT.rearrange("p (fc o) -> p fc o", fc=H, o=D)
    zt = [arB[:, 0:2048], arB[:, 2048:4096]]
    zsl = [slots["z0"], slots["z1"]]
    x1t = arD[:, 4096:6144]
    outb = arD[:, 4096:6144]
    zt_read = [None, None]
    psbig_reader3 = None
    x1_store = None
    x1_stores = {}
    astage = [arE[:, 0:2048], arE[:, 2048:4096]]
    astage_read = [None, None]
    def p3_loads(t):
        par = t % 2
        war = (zt_read[par],) if zt_read[par] else ()
        tz = SD.dma(zsl[par], lambda e, t=t, par=par: e.dma_start(
            out=zt[par].rearrange("p (hh c) -> p hh c", c=128),
            in_=zs[t]), waits=war)
        warx = (xt_read[par],) if xt_read[par] else ()
        tx = SD.dma(xsl[par], lambda e, t=t, par=par: e.dma_start(
            out=xt[par][:, :], in_=x_h[t * 128:(t + 1) * 128, :]), waits=warx)
        return tz, tx

    ld_pend = {0: p3_loads(0)}
    for t in range(NT):
        par = t % 2
        tz, tx = ld_pend.pop(t)
        if t + 1 < NT:
            ld_pend[t + 1] = p3_loads(t + 1)
        PE.wait(tz)
        PE.wait(wo_toks[-1])
        PE.wait(psbig_reader3)
        stg = astage[par]
        tev = None
        for dc in range(4):
            tmm = None
            for fc in range(H):
                tmm = PE.opi(lambda e, dc=dc, fc=fc, par=par: e.matmul(
                    PS_BIG[:, dc * 512:(dc + 1) * 512],
                    zt[par][:, fc * 128:(fc + 1) * 128],
                    woT[:, fc * 2048 + dc * 512: fc * 2048 + (dc + 1) * 512],
                    start=(fc == 0), stop=(fc == H - 1)))
            AE.wait(tmm)
            AE.wait(astage_read[par])
            tev = AE.opi(lambda e, dc=dc, stg=stg: e.copy(
                stg[:, dc * 512:(dc + 1) * 512], PS_BIG[:, dc * 512:(dc + 1) * 512]))
        zt_read[par] = tmm
        psbig_reader3 = tev
        nmu, rsd, rdy = emit_ln(stg, par, pre_toks=(tev,))
        VE.waits(rdy)
        VE.wait(x1_store)
        tts = VE.opi(lambda e, nmu=nmu, rsd=rsd, stg=stg: e.tensor_scalar(
            x1t[:, :], stg[:, :], nmu[:], rsd[:], ALU.add, ALU.mult))
        astage_read[par] = tts
        VE.wait(tx)
        VE.wait(tts)
        tadd = VE.opi(lambda e, par=par: e.tensor_add(
            x1t[:, :], x1t[:, :], xt[par][:, :]))
        xt_read[par] = tadd
        GD.wait(tadd)
        x1_store = GD.dma(slots["gwr"], lambda e, t=t: e.dma_start(
            out=x1s[t * 128:(t + 1) * 128, :], in_=x1t[:, :]))
        x1_stores[t] = x1_store

    barrier()

    h_sb = arA
    h_3d = h_sb.rearrange("p (it r) -> p it r", it=NIT, r=512)
    wslab = [arB[:, i * 4096:(i + 1) * 4096] for i in range(4)]
    wslab_slots = [slots["wg0"], slots["wg1"], slots["wu0"], slots["wu1"]]
    wx = arC[:, 9216:17408]
    wx_g, wx_u = wx[:, 0:4096], wx[:, 4096:8192]
    sg = [arC[:, 8192:8704], arC[:, 8704:9216]]
    y_tm = arE[:, 0:8192]
    y_3d = y_tm.rearrange("p (s r) -> p s r", s=4, r=2048)
    y_fm = [arE[:, 8192:8704], arE[:, 8704:9216]]

    wslab_read = [None, None, None, None]
    wx_read = None
    wx_toks = [None, None]
    wd0_tok = None
    sg_war = [None, None]
    h_read = None
    yfm_read = [None, None]
    ytm_read = None
    out_store = None
    outb_war = None
    IPS = ISLAB // 128

    def load_wx_sl0(war):
        t1 = GD.dma(slots["wx0"], lambda e: e.dma_start(
            out=wx_g.rearrange("p (dc i) -> p dc i", i=ISLAB), in_=wg_h[0]),
            waits=war)
        t2 = GD.dma(slots["wx1"], lambda e: e.dma_start(
            out=wx_u.rearrange("p (dc i) -> p dc i", i=ISLAB), in_=wu_h[0]))
        return [t1, t2]

    wx_toks = load_wx_sl0(())
    ln2_ready = emit_ln2_stage(0)
    for T in range(NTS):
        if debug and T == 0:
            SD.wait(ln2_ready)
            SD.dma(slots["swr"], lambda e: e.dma_start(out=t_ln2, in_=ln2_fm[:, :]))
        PE.wait(ln2_ready)
        PE.wait(t_ones)
        PE.wait(bank_war["A"])
        tls = None
        for dc in range(NDC):
            tls = PE.opi(lambda e, dc=dc: e.matmul(
                PS_A[0:1, :], ones16[:, 0:1], ln2_fm[:, dc * 512:(dc + 1) * 512],
                start=(dc == 0), stop=(dc == NDC - 1)))
        AE.wait(tls)
        tnl = AE.opi(lambda e: e.mul(negln[:], PS_A[0:1, :], -128.0))
        bank_war["A"] = tnl

        def load_wslab(m, sl):
            buf = 2 * m + (sl % 2)
            war = (wslab_read[buf],) if wslab_read[buf] else ()
            src = wg_h if m == 0 else wu_h
            return GD.dma(wslab_slots[buf], lambda e, sl=sl, buf=buf, src=src:
                          e.dma_start(out=wslab[buf].rearrange(
                              "p (dc i) -> p dc i", i=ISLAB), in_=src[sl]),
                          waits=war)

        pend = {(0, 1): load_wslab(0, 1), (1, 1): load_wslab(1, 1)}
        hdone = None
        for sl in range(NSLAB):
            if sl == 0:
                gtok, utok = wx_toks
                gsrc, usrc = wx_g, wx_u
            else:
                gtok = pend.pop((0, sl))
                utok = pend.pop((1, sl))
                gsrc = wslab[0 + (sl % 2)]
                usrc = wslab[2 + (sl % 2)]
            if sl + 1 < NSLAB:
                pend[(0, sl + 1)] = load_wslab(0, sl + 1)
                pend[(1, sl + 1)] = load_wslab(1, sl + 1)
            if sl == 1:
                war = [wx_read] if wx_read else []
                wd0_tok = GD.dma(slots["wx0"], lambda e, T=T: e.dma_start(
                    out=wx.rearrange("p (ic r) -> p ic r", r=128), in_=wd_h[0]),
                    waits=war)
            for ir in range(IPS):
                it = sl * IPS + ir
                par = it % 2
                PE.wait(gtok)
                PE.wait(bank_war["A"])
                for dc in range(NDC):
                    PE.opi(lambda e, gsrc=gsrc, dc=dc, ir=ir: e.matmul(
                        PS_A[:, :],
                        gsrc[:, dc * ISLAB + ir * 128: dc * ISLAB + (ir + 1) * 128],
                        ln2_fm[:, dc * 512:(dc + 1) * 512], start=(dc == 0), stop=False))
                gmm = PE.opi(lambda e: e.matmul(
                    PS_A[:, :], ones16[0:1, 0:128], negln[:],
                    start=False, stop=True, skip_group_check=True))
                PE.wait(utok)
                PE.wait(bank_war["Z"])
                for dc in range(NDC):
                    PE.opi(lambda e, usrc=usrc, dc=dc, ir=ir: e.matmul(
                        PS_Z[:, :],
                        usrc[:, dc * ISLAB + ir * 128: dc * ISLAB + (ir + 1) * 128],
                        ln2_fm[:, dc * 512:(dc + 1) * 512], start=(dc == 0), stop=False))
                umm = PE.opi(lambda e: e.matmul(
                    PS_Z[:, :], ones16[0:1, 0:128], negln[:],
                    start=False, stop=True, skip_group_check=True))
                ln2_read = umm
                if sl == 0:
                    wx_read = umm
                else:
                    wslab_read[0 + (sl % 2)] = gmm
                    wslab_read[2 + (sl % 2)] = umm
                AE.wait(gmm)
                AE.wait(sg_war[par])
                tsi = AE.opi(lambda e, it=it, par=par: e.activation(
                    sg[par][:, :], PS_A[:, :], AF.Silu, scale=gsc[:, it:it + 1]))
                bank_war["A"] = tsi
                VE.wait(umm)
                VE.wait(tsi)
                VE.wait(h_read)
                th = VE.opi(lambda e, it=it, par=par: e.scalar_tensor_tensor(
                    h_sb[:, it * 512:(it + 1) * 512], PS_Z[:, :], usc[:, it:it + 1],
                    sg[par][:, :], ALU.mult, ALU.mult))
                bank_war["Z"] = th
                sg_war[par] = th
                hdone = th
        h_read = None
        if debug and T == 0:
            SD.wait(hdone)
            SD.dma(slots["swr"], lambda e: e.dma_start(out=t_h, in_=h_sb[:, :]))
        PE.wait(hdone)
        PE.wait(bank_war["A"])
        ths = None
        for ic in range(NIT):
            ths = PE.opi(lambda e, ic=ic: e.matmul(
                PS_A[0:1, :], ones16[:, 0:1], h_sb[:, ic * 512:(ic + 1) * 512],
                start=(ic == 0), stop=(ic == NIT - 1)))
        AE.wait(ths)
        tnh = AE.opi(lambda e: e.mul(neghs[:], PS_A[0:1, :], -128.0))
        bank_war["A"] = tnh

        wd_read = [None, None]

        def load_wd(dt):
            par = dt % 2
            b0 = 0 if par == 0 else 2
            war = []
            if wd_read[par] is not None:
                war.append(wd_read[par])
            if dt < 3:
                for b in (b0, b0 + 1):
                    if wslab_read[b] is not None:
                        war.append(wslab_read[b])
            wdv = arB[:, b0 * 4096: b0 * 4096 + 8192]
            tok = GD.dma(wslab_slots[b0], lambda e, dt=dt, wdv=wdv: e.dma_start(
                out=wdv.rearrange("p (ic r) -> p ic r", r=128), in_=wd_h[dt]),
                waits=war)
            return tok

        wd_pend = {1: load_wd(1)}
        last_yt = None
        for dt in range(NDC):
            if dt == 0:
                wtok = wd0_tok
                wdv = wx
            else:
                wtok = wd_pend.pop(dt)
                b0 = 0 if dt % 2 == 0 else 2
                wdv = arB[:, b0 * 4096: b0 * 4096 + 8192]
            if dt + 1 < NDC:
                wd_pend[dt + 1] = load_wd(dt + 1)
            par = dt % 2
            bk = "A" if par == 0 else "Z"
            if dt == 1:
                if T + 1 < NTS:
                    nonlocal_war = [h_read] if h_read else []
                    wx_toks[:] = [
                        GD.dma(slots["wx0"], lambda e: e.dma_start(
                            out=wx_g.rearrange("p (dc i) -> p dc i", i=ISLAB),
                            in_=wg_h[0]), waits=nonlocal_war),
                        GD.dma(slots["wx1"], lambda e: e.dma_start(
                            out=wx_u.rearrange("p (dc i) -> p dc i", i=ISLAB),
                            in_=wu_h[0]))]
            PE.wait(wtok)
            PE.wait(t_ones32)
            PE.wait(bank_war[bk])
            for ic in range(NIT):
                PE.opi(lambda e, bk=bk, wdv=wdv, ic=ic: e.matmul(
                    projb[bk][:, :], wdv[:, ic * 128:(ic + 1) * 128],
                    h_sb[:, ic * 512:(ic + 1) * 512], start=(ic == 0), stop=False))
            tdm = PE.opi(lambda e, bk=bk: e.matmul(
                projb[bk][:, :], ones32r[:], neghs[:],
                start=False, stop=True, skip_group_check=True))
            h_read = tdm
            wd_read[par] = tdm
            if dt == 0:
                wx_read = tdm
            else:
                wslab_read[2 * par] = tdm
                wslab_read[2 * par + 1] = tdm
            AE.wait(tdm)
            AE.wait(yfm_read[par])
            tye = AE.opi(lambda e, bk=bk, dt=dt, par=par: e.mul(
                y_fm[par][:, :], projb[bk][:, :], dsc[:, dt:dt + 1]))
            bank_war[bk] = tye
            srcs = [y_fm[par][:, i * 128:(i + 1) * 128] for i in range(4)]
            dst3 = y_3d[:, :, dt * 128:(dt + 1) * 128]
            last_yt = transpose4(srcs, dst3, src_tok=tye,
                                 dst_war=(ytm_read,) if ytm_read else (), f32=True)
            yfm_read[par] = last_yt
        if debug and T == 0:
            SD.wait(last_yt)
            SD.dma(slots["swr"], lambda e: e.dma_start(out=t_y, in_=y_tm[:, 0:2048]))
        if T + 1 < NTS:
            ln2_ready = emit_ln2_stage(T + 1)
        for tt in range(4):
            t_abs = 4 * T + tt
            par = tt % 2
            warx = [(slots["gwr"], slots["gwr"].cnt)]
            if xt_read[par]:
                warx.append(xt_read[par])
            tx = SD.dma(xsl[par], lambda e, t_abs=t_abs, par=par: e.dma_start(
                out=x1a[par][:, :], in_=x1s[t_abs * 128:(t_abs + 1) * 128, :]),
                waits=warx)
            ytile = y_tm[:, tt * 2048:(tt + 1) * 2048]
            nmu, rsd, rdy = emit_ln(ytile, par, pre_toks=(last_yt,))
            VE.waits(rdy)
            VE.wait(outb_war)
            tts = VE.opi(lambda e, ytile=ytile, nmu=nmu, rsd=rsd: e.tensor_scalar(
                outb[:, :], ytile, nmu[:], rsd[:], ALU.add, ALU.mult))
            ytm_read = tts
            VE.wait(tx)
            VE.wait(tts)
            tadd = VE.opi(lambda e, par=par: e.tensor_add(
                x1a[par][:, :], x1a[par][:, :], outb[:, :]))
            outb_war = tadd
            GD.wait(tadd)
            out_store = GD.dma(slots["gwr"], lambda e, t_abs=t_abs, par=par:
                               e.dma_start(
                out=out_h[t_abs * 128:(t_abs + 1) * 128, :], in_=x1a[par][:, :]))
            xt_read[par] = out_store

    GD.wait(out_store)
    barrier()

    with nc.Block() as block:
        def mk(stream):
            def runner(eng):
                for item in stream.ops:
                    k = item[0]
                    if k == "w":
                        eng.wait_ge(item[1].sem, item[2])
                    elif k == "o":
                        item[1](eng)
                    elif k == "i":
                        item[1](eng).then_inc(stream.sem, stream.unit)
                    else:
                        item[1](eng).then_inc(item[2].sem, 16)
            return runner
        block.tensor(mk(PE))
        block.vector(mk(VE))
        block.scalar(mk(AE))
        block.gpsimd(mk(GD))
        block.sync(mk(SD))

    es.close()
    return nc


def _prep(L0_attn_x, L0_attn_W_K, L0_attn_W_Q, L0_attn_W_V, L0_attn_W_O,
          L0_ffn_gate_w, L0_ffn_gate_scale, L0_ffn_up_w, L0_ffn_up_scale,
          L0_ffn_down_w, L0_ffn_down_scale):
    c = np.ascontiguousarray
    f32 = np.float32
    x = np.asarray(L0_attn_x, dtype=f32)

    def wfmt(w):
        w = np.asarray(w, dtype=f32)
        return c(w.transpose(0, 2, 1).reshape(H, NDC, 128, DH).transpose(0, 2, 1, 3))

    wqH = wfmt(L0_attn_W_Q)
    wkH = wfmt(L0_attn_W_K)
    wvH = wfmt(L0_attn_W_V)
    woH = c(np.asarray(L0_attn_W_O, dtype=f32).T.reshape(NDC, 128, D).transpose(1, 0, 2))

    ffn_u8 = all(int(np.asarray(w).min()) >= 0 and int(np.asarray(w).max()) <= 255
                 for w in (L0_ffn_gate_w, L0_ffn_up_w, L0_ffn_down_w))
    qdt = np.uint8 if ffn_u8 else np.int32

    def gfmt(w):
        w = np.asarray(w, dtype=np.int32).astype(qdt)
        return c(w.reshape(NSLAB, ISLAB, NDC, 128).transpose(0, 3, 2, 1))

    wgH = gfmt(L0_ffn_gate_w)
    wuH = gfmt(L0_ffn_up_w)
    wdH = c(np.asarray(L0_ffn_down_w, dtype=np.int32).astype(qdt)
            .reshape(NDC, 128, NIT, 128).transpose(0, 3, 2, 1))
    gscH = c(np.asarray(L0_ffn_gate_scale, dtype=f32).reshape(NIT, 128).T)
    uscH = c(np.asarray(L0_ffn_up_scale, dtype=f32).reshape(NIT, 128).T)
    dscH = c(np.asarray(L0_ffn_down_scale, dtype=f32).reshape(NDC, 128).T)
    id16H = np.eye(128, dtype=bfdt)
    id32H = np.eye(128, dtype=f32)
    r = np.arange(128)
    cmaskH = np.where(r[:, None] >= r[None, :], 0.0, NEG_BIG).astype(f32)
    shared = dict(wqH=wqH, wkH=wkH, wvH=wvH, woH=woH, wgH=wgH, wuH=wuH, wdH=wdH,
                  gscH=gscH, uscH=uscH, dscH=dscH, id16H=id16H, id32H=id32H,
                  cmaskH=cmaskH)
    return x, shared, ffn_u8


def kernel(**inputs):
    x, shared, ffn_u8 = _prep(**inputs)
    nc = build(debug=DEBUG, ffn_u8=ffn_u8)
    in_maps = [dict(shared, x=np.ascontiguousarray(x[cid])) for cid in range(8)]
    res = run_bass_kernel_spmd(nc, in_maps, core_ids=list(range(8)))
    return np.stack([res.results[cid]["out"] for cid in range(8)], axis=0)


# version 15
# speedup vs baseline: 1.0836x; 1.0836x over previous
import numpy as np
import ml_dtypes
import concourse.bass as bass
import concourse.mybir as mybir
from concourse.bass_utils import run_bass_kernel_spmd
from contextlib import ExitStack

F32 = mybir.dt.float32
BF16 = mybir.dt.bfloat16
I32 = mybir.dt.int32
U8 = mybir.dt.uint8
AF = mybir.ActivationFunctionType
AX = mybir.AxisListType
ALU = mybir.AluOpType
bfdt = ml_dtypes.bfloat16

DEBUG = False

D = 2048
S = 2048
H = 16
DH = 128
II = 8192
NT = S // 128
NDC = D // 128
NIT = II // 128
NQ = 4
TS = 512
NTS = S // TS
ISLAB = 256
NSLAB = II // ISLAB
INV_SQRT_DH = 0.08838834764831845
NEG_BIG = -1.0e9
EPS = 1e-5


class Stream:
    def __init__(self, name, unit):
        self.name = name
        self.unit = unit
        self.sem = None
        self.cnt = 0
        self.ops = []
        self.marks = {}
        self.covered = True

    def op(self, fn):
        self.ops.append(("o", fn))
        self.covered = False

    def opi(self, fn):
        self.cnt += self.unit
        self.ops.append(("i", fn))
        self.covered = True
        return (self, self.cnt)

    def dma(self, slot, fn, waits=()):
        for t in waits:
            self.wait(t)
        slot.cnt += 16
        self.ops.append(("d", fn, slot))
        self.covered = True
        return (slot, slot.cnt)

    def wait(self, tok):
        if tok is None:
            return
        src, val = tok
        if src is self and self.cnt < val:
            raise RuntimeError(f"{self.name}: wait on own future {val}>{self.cnt}")
        if self.marks.get(src.name, 0) >= val:
            return
        self.marks[src.name] = val
        self.ops.append(("w", src, val))

    def waits(self, toks):
        for t in toks:
            self.wait(t)


class Slot:
    def __init__(self, name):
        self.name = name
        self.sem = None
        self.cnt = 0


def build(debug=False, ffn_u8=True):
    nc = bass.Bass("TRN2", target_bir_lowering=False)

    x_h = nc.dram_tensor("x", [S, D], F32, kind="ExternalInput").ap()
    wq_h = nc.dram_tensor("wqH", [H, 128, NDC, 128], F32, kind="ExternalInput").ap()
    wk_h = nc.dram_tensor("wkH", [H, 128, NDC, 128], F32, kind="ExternalInput").ap()
    wv_h = nc.dram_tensor("wvH", [H, 128, NDC, 128], F32, kind="ExternalInput").ap()
    wo_h = nc.dram_tensor("woH", [128, NDC, D], F32, kind="ExternalInput").ap()
    WQDT = U8 if ffn_u8 else I32
    wg_h = nc.dram_tensor("wgH", [NSLAB, 128, NDC, ISLAB], WQDT, kind="ExternalInput").ap()
    wu_h = nc.dram_tensor("wuH", [NSLAB, 128, NDC, ISLAB], WQDT, kind="ExternalInput").ap()
    wd_h = nc.dram_tensor("wdH", [NDC, 128, NIT, 128], WQDT, kind="ExternalInput").ap()
    gsc_h = nc.dram_tensor("gscH", [128, NIT], F32, kind="ExternalInput").ap()
    usc_h = nc.dram_tensor("uscH", [128, NIT], F32, kind="ExternalInput").ap()
    dsc_h = nc.dram_tensor("dscH", [128, NDC], F32, kind="ExternalInput").ap()
    id16_h = nc.dram_tensor("id16H", [128, 128], BF16, kind="ExternalInput").ap()
    id32_h = nc.dram_tensor("id32H", [128, 128], F32, kind="ExternalInput").ap()
    cmask_h = nc.dram_tensor("cmaskH", [128, 128], F32, kind="ExternalInput").ap()
    out_h = nc.dram_tensor("out", [S, D], F32, kind="ExternalOutput").ap()
    if debug:
        x1s = nc.dram_tensor("x1s", [S, D], F32, kind="ExternalOutput").ap()
        zs = nc.dram_tensor("zs", [NT, 128, H, 128], BF16, kind="ExternalOutput").ap()
    else:
        x1s = nc.dram_tensor("x1s", [S, D], F32).ap()
        zs = nc.dram_tensor("zs", [NT, 128, H, 128], BF16).ap()

    def tapdecl(name, shape, dtype):
        if debug:
            return nc.dram_tensor(name, shape, dtype, kind="ExternalOutput").ap()
        return None

    t_ln1 = tapdecl("t_ln1", [128, NDC * S], BF16)
    t_kf = tapdecl("t_kf", [128, S], BF16)
    t_qf = tapdecl("t_qf", [128, S], BF16)
    t_vt = tapdecl("t_vt", [128, S], BF16)
    t_p = tapdecl("t_p", [128, S], BF16)
    t_pa = tapdecl("t_pa", [128, S], BF16)
    t_ln2 = tapdecl("t_ln2", [128, NDC * TS], BF16)
    t_h = tapdecl("t_h", [128, NIT * TS], BF16)
    t_y = tapdecl("t_y", [128, D], F32)

    PE = Stream("pe", 1)
    VE = Stream("ve", 1)
    AE = Stream("ae", 1)
    GD = Stream("gq", 16)
    SD = Stream("sq", 16)
    ALLS = (PE, VE, AE, GD, SD)
    slots = {n: Slot(n) for n in
             ["wg0", "wg1", "wu0", "wu1", "wqkv", "x0", "x1", "z0", "z1",
              "swr", "gwr", "stg", "wx0", "wx1"]}

    es = ExitStack()
    sb = lambda name, shape, dt: es.enter_context(nc.sbuf_tensor(name, shape, dt))
    psal = lambda name, shape, dt: es.enter_context(nc.psum_tensor(name, shape, dt))

    id16 = sb("id16", [128, 128], BF16)
    id32 = sb("id32", [128, 128], F32)
    cmask = sb("cmask", [128, 128], F32)
    ones16 = sb("ones16", [128, 128], BF16)
    ones32r = sb("ones32r", [1, 128], F32)
    epsv = sb("epsv", [128, 1], F32)
    gsc = sb("gsc", [128, NIT], F32)
    usc = sb("usc", [128, NIT], F32)
    dsc = sb("dsc", [128, NDC], F32)
    arA = sb("arA", [128, 32768], BF16)
    arB = sb("arB", [128, 16384], BF16)
    arC = sb("arC", [128, 22528], BF16)
    arD = sb("arD", [128, 6144], F32)
    arE = sb("arE", [128, 9216], F32)
    bns = [sb(f"bns{i}", [128, 24], F32) for i in range(2)]
    mv = [sb(f"mv{i}", [128, 2], F32) for i in range(2)]
    sdv = [sb(f"sd{i}", [128, 1], F32) for i in range(2)]
    rstd = [sb(f"rstd{i}", [128, 1], F32) for i in range(2)]
    negmu = [sb(f"negmu{i}", [128, 1], F32) for i in range(2)]
    nbv = [sb(f"nb{i}", [128, 1], F32) for i in range(2)]
    sev = [sb(f"se{i}", [128, 1], F32) for i in range(2)]
    rcv = [sb(f"rc{i}", [128, 1], F32) for i in range(2)]
    mxv = [sb(f"mx{i}", [128, 1], F32) for i in range(2)]
    negln = sb("negln", [1, TS], BF16)
    stgscr = sb("stgscr", [1, 4], F32)
    neghs = sb("neghs", [1, TS], F32)

    def lntm(p):
        return arC[:, 18432 + p * 2048: 18432 + (p + 1) * 2048]

    PS_BIG = psal("psbig", [128, 2048], F32)
    PS_A = psal("psa", [128, 512], F32)
    PS_Z = psal("psz", [128, 512], F32)
    PS_T = psal("pst", [128, 2048], BF16)

    for st in ALLS:
        st.sem = es.enter_context(nc.semaphore(f"sem_{st.name}"))
    for n, sl in slots.items():
        sl.sem = es.enter_context(nc.semaphore(f"sem_{n}"))

    bank_war = {"A": None, "Z": None}

    def barrier():
        for st in ALLS:
            if not st.covered and st.ops:
                raise RuntimeError(f"barrier: {st.name} tail uncovered")
        toks = [(st, st.cnt) for st in ALLS] + \
               [(sl, sl.cnt) for sl in slots.values()]
        for st in ALLS:
            for t in toks:
                if t[0] is not st:
                    st.wait(t)

    GD.dma(slots["wqkv"], lambda e: e.dma_start(out=id16[:], in_=id16_h))
    GD.dma(slots["wqkv"], lambda e: e.dma_start(out=id32[:], in_=id32_h))
    GD.dma(slots["wqkv"], lambda e: e.dma_start(out=cmask[:], in_=cmask_h))
    GD.dma(slots["wqkv"], lambda e: e.dma_start(out=gsc[:], in_=gsc_h))
    GD.dma(slots["wqkv"], lambda e: e.dma_start(out=usc[:], in_=usc_h))
    c_last = GD.dma(slots["wqkv"], lambda e: e.dma_start(out=dsc[:], in_=dsc_h))
    t_ones = VE.opi(lambda e: e.memset(ones16[:], 1.0))
    t_ones32 = VE.opi(lambda e: e.memset(ones32r[:], 1.0))
    t_eps = VE.opi(lambda e: e.memset(epsv[:], EPS))

    def emit_ln(src_ap, par, pre_toks=()):
        b, m = bns[par], mv[par]
        VE.waits(pre_toks)
        VE.wait(t_eps)
        tb = None
        for c in range(4):
            tb = VE.opi(lambda e, b=b, c=c: e.bn_stats(
                b[:, 6 * c:6 * (c + 1)], src_ap[:, 512 * c:512 * (c + 1)]))
        VE.wait(tb)
        ta = VE.opi(lambda e, b=b, m=m: e.bn_aggr(m[:], b[:]))
        AE.wait(ta)
        ts_ = AE.opi(lambda e, m=m, par=par: e.activation(
            sdv[par][:], m[:, 1:2], AF.Sqrt, bias=epsv[:], scale=1.0))
        VE.wait(ts_)
        tr = VE.opi(lambda e, par=par: e.reciprocal(rstd[par][:], sdv[par][:]))
        tn = VE.opi(lambda e, m=m, par=par: e.tensor_scalar_mul(
            negmu[par][:], m[:, 0:1], -1.0))
        return negmu[par], rstd[par], (tr, tn)

    tr_half = [0]
    tr_evac = [None, None]
    tr_alt = [0]
    tr_cover = {"ve": None, "ae": None}

    def transpose4(srcs, dst3, src_tok=None, dst_war=(), f32=False, alt=False):
        n = len(srcs)
        half = tr_half[0]
        tr_half[0] ^= 1
        if f32:
            base = 512 + half * 512 if False else half * 512
            pst, ident = PS_BIG, id32
        else:
            base = half * 1024
            pst, ident = PS_T, id16
        if src_tok is not None:
            PE.wait(src_tok)
        if tr_evac[half] is not None:
            PE.wait(tr_evac[half])
        tt = None
        for i, s_ap in enumerate(srcs):
            tt = PE.opi(lambda e, s_ap=s_ap, i=i, pst=pst, base=base, ident=ident:
                        e.transpose(pst[:, base + i * 128: base + (i + 1) * 128],
                                    s_ap, ident[:]))
        if alt:
            tr_alt[0] ^= 1
            ev = VE if tr_alt[0] else AE
        else:
            ev = AE
        ev.wait(tt)
        ev.waits(dst_war)
        src3 = pst[:, base: base + n * 128].rearrange("p (n r) -> p n r", n=n, r=128)
        if ev is AE:
            tok = ev.opi(lambda e, dst3=dst3, src3=src3: e.copy(dst3, src3))
            tr_cover["ae"] = tok
        else:
            tok = ev.opi(lambda e, dst3=dst3, src3=src3: e.tensor_copy(dst3, src3))
            tr_cover["ve"] = tok
        tr_evac[half] = tok
        return tok


    ln1_fm = arA
    ln1_3d = ln1_fm.rearrange("p (dc r) -> p dc r", dc=NDC, r=S)
    wq_sb, wk_sb, wv_sb = arB[:, 0:2048], arB[:, 2048:4096], arB[:, 4096:6144]
    vtmp = [arB[:, 6144:6656], arB[:, 6656:7168]]
    z_sb = arB[:, 7168:9216]
    arEb = arE[:, :].bitcast(BF16)
    kset = [(arC[:, 0:2048], arC[:, 2048:4096], arC[:, 4096:6144]),
            (arEb[:, 0:2048], arEb[:, 2048:4096], arEb[:, 4096:6144])]
    p_sb = [arC[:, 6144:8192], arC[:, 8192:10240],
            arEb[:, 6144:8192], arEb[:, 8192:10240]]
    pT = arC[:, 10240:18432]
    pT_3d = pT.rearrange("p (kc r) -> p kc r", kc=16, r=512)
    SC = [PS_BIG[:, 0:1024], PS_BIG[:, 1024:2048]]
    projb = {"A": PS_A, "Z": PS_Z}
    sm2 = [sb(f"sm2{i}", [128, 16], F32) for i in range(4)]
    MX0, MX1, M01, NB0, NB1, SE0, SE1, D0, E0, S0E0, SSUM, RC, F0 = range(13)

    wqkv_read = [None]
    z_store_tok = None
    region_free = [None, None]
    pv_prev = None
    zevac_prev = None
    p_war = [None] * 4
    chain_war = [None] * 4
    proj_final = [None, None]

    def load_head_weights(h, war):
        GD.dma(slots["wqkv"], lambda e, h=h: e.dma_start(
            out=wq_sb.rearrange("p (dc e) -> p dc e", e=128), in_=wq_h[h]),
            waits=war)
        GD.dma(slots["wqkv"], lambda e, h=h: e.dma_start(
            out=wk_sb.rearrange("p (dc e) -> p dc e", e=128), in_=wk_h[h]))
        return GD.dma(slots["wqkv"], lambda e, h=h: e.dma_start(
            out=wv_sb.rearrange("p (dc e) -> p dc e", e=128), in_=wv_h[h]))

    wload = [load_head_weights(0, (c_last,))]

    def make_proj_chunks(h):
        par = h % 2
        kf_, qf_, vt_ = kset[par]
        vt3 = vt_.rearrange("p (kc r) -> p kc r", kc=16, r=128)
        chunks = []

        def mk_half(wsb, tc, half):
            def em():
                PE.wait(wload[0])
                if half == 0:
                    PE.wait(bank_war["A"])
                for dc in range(8 * half, 8 * half + 8):
                    tmm = PE.opi(lambda e, wsb=wsb, dc=dc, tc=tc: e.matmul(
                        PS_A[:, :], wsb[:, dc * 128:(dc + 1) * 128],
                        ln1_fm[:, dc * 2048 + tc * 512: dc * 2048 + (tc + 1) * 512],
                        start=(dc == 0), stop=(dc == NDC - 1)))
                return tmm
            return em

        def mk_kq_b(wsb, dstfm, tc, scale):
            emh = mk_half(wsb, tc, 1)

            def em():
                tmm = emh()
                wqkv_read[0] = tmm
                AE.wait(tmm)
                if scale is None:
                    tev = AE.opi(lambda e, dstfm=dstfm, tc=tc: e.copy(
                        dstfm[:, tc * 512:(tc + 1) * 512], PS_A[:, :]))
                else:
                    tev = AE.opi(lambda e, dstfm=dstfm, tc=tc, scale=scale: e.mul(
                        dstfm[:, tc * 512:(tc + 1) * 512], PS_A[:, :], scale))
                bank_war["A"] = tev
                proj_final[par] = tev
            return em

        def mk_v_b(tc):
            emh = mk_half(wv_sb, tc, 1)

            def em():
                tmm = emh()
                wqkv_read[0] = tmm
                AE.wait(tmm)
                tev = AE.opi(lambda e, tc=tc: e.copy(
                    vtmp[tc % 2][:, :], PS_A[:, :]))
                bank_war["A"] = tev
                srcs = [vtmp[tc % 2][:, i * 128:(i + 1) * 128] for i in range(4)]
                tvt = transpose4(srcs, vt3[:, 4 * tc:4 * tc + 4, :], src_tok=tev,
                                 alt=True)
                bank_war["A"] = tvt
                proj_final[par] = tvt
            return em

        for tc in range(4):
            chunks.append(mk_half(wk_sb, tc, 0))
            chunks.append(mk_kq_b(wk_sb, kf_, tc, None))
        for tc in range(4):
            chunks.append(mk_half(wq_sb, tc, 0))
            chunks.append(mk_kq_b(wq_sb, qf_, tc, INV_SQRT_DH))
        for tc in range(4):
            chunks.append(mk_v(tc)) if False else None
            chunks.append(mk_half(wv_sb, tc, 0))
            chunks.append(mk_v_b(tc))
        return chunks


    h0_chunks = make_proj_chunks(0)
    for _st in (PE, VE, AE):
        _st.wait((slots["wqkv"], slots["wqkv"].cnt))
    xt = [arD[:, 0:2048], arD[:, 2048:4096]]
    xsl = [slots["x0"], slots["x1"]]
    xt_read = [None, None]
    lntm_read = [None, None]
    last_ln1_tok = None
    for t in range(NT):
        par = t % 2
        war = (xt_read[par],) if xt_read[par] else ()
        tx = SD.dma(xsl[par], lambda e, t=t, par=par: e.dma_start(
            out=xt[par][:, :], in_=x_h[t * 128:(t + 1) * 128, :]), waits=war)
        nmu, rsd, rdy = emit_ln(xt[par], par, pre_toks=(tx,))
        VE.waits(rdy)
        VE.wait(lntm_read[par])
        ltm = lntm(par)
        tnorm = VE.opi(lambda e, par=par, nmu=nmu, rsd=rsd: e.tensor_scalar(
            lntm(par)[:, :], xt[par][:, :], nmu[:], rsd[:], ALU.add, ALU.mult))
        xt_read[par] = tnorm
        for g in range(2):
            srcs = [ltm[:, (8 * g + i) * 128:(8 * g + i + 1) * 128] for i in range(8)]
            dst3 = ln1_3d[:, 8 * g:8 * g + 8, t * 128:(t + 1) * 128]
            last_ln1_tok = transpose4(srcs, dst3, src_tok=tnorm)
        lntm_read[par] = last_ln1_tok
        if t % 4 == 3:
            tc = t // 4
            PE.wait(last_ln1_tok)
            for ci in (2 * tc, 2 * tc + 1, 8 + 2 * tc, 8 + 2 * tc + 1,
                       16 + 2 * tc, 16 + 2 * tc + 1):
                h0_chunks[ci]()
    if debug:
        SD.wait(last_ln1_tok)
        SD.dma(slots["swr"], lambda e: e.dma_start(out=t_ln1, in_=ln1_fm[:, :]))

    barrier()

    wload[0] = load_head_weights(1, (wqkv_read[0],))

    pass_idx = [0]

    woT_3dE = arA.rearrange("p (fc o) -> p fc o", fc=H, o=D)
    wo_toks = []
    for h in range(H):
        par_h = h % 2
        kf_, qf_, vt_ = kset[par_h]
        PE.wait(proj_final[par_h])
        PE.wait(tr_cover["ve"])
        PE.wait(tr_cover["ae"])
        if h == H - 1:
            for g in range(4):
                wo_toks.append(GD.dma(slots["wqkv"], lambda e, g=g: e.dma_start(
                    out=woT_3dE[:, 4 * g:4 * (g + 1), :],
                    in_=wo_h[:, 4 * g:4 * (g + 1), :]),
                    waits=(wqkv_read[0],) if g == 0 else ()))
        fillers = make_proj_chunks(h + 1) if h + 1 < H else []
        fill_i = 0
        pending_tr = []
        half_pend = [None] * 4
        tms_cur = [None]

        def emit_transposes(qt, ctok, h=h, par_h=par_h):
            parq = qt % 4
            if qt % 4 == 0:
                Q = qt // 4
                VE.wait(pv_prev)
                tms = None
                for kc in range(4 * Q, 4 * Q + 4):
                    tms = VE.opi(lambda e, kc=kc: e.memset(
                        pT[:, kc * 512:(kc + 1) * 512], 0.0))
                tms_cur[0] = tms
            last_tr = None
            j = qt % 4
            for g in range((qt + 8) // 8):
                k0 = 8 * g
                nk = min(8, qt + 1 - k0)
                srcs = [p_sb[parq][:, (k0 + i) * 128:(k0 + i + 1) * 128]
                        for i in range(nk)]
                dst3 = pT_3d[:, k0:k0 + nk, j * 128:(j + 1) * 128]
                last_tr = transpose4(srcs, dst3, src_tok=ctok,
                                     dst_war=(tms_cur[0],), alt=True)
            p_war[parq] = last_tr
            return last_tr

        def emit_pv(Q, h=h, vt_=vt_):
            nonlocal pv_prev, zevac_prev, z_store_tok
            nkc = 4 * (Q + 1)
            PE.wait(p_war[0])
            PE.wait(p_war[1])
            PE.wait(tr_cover["ve"])
            PE.wait(tr_cover["ae"])
            PE.wait(bank_war["Z"])
            tpv = None
            for kc in range(nkc):
                tpv = PE.opi(lambda e, kc=kc, nkc=nkc: e.matmul(
                    PS_Z[:, :], vt_[:, kc * 128:(kc + 1) * 128],
                    pT[:, kc * 512:(kc + 1) * 512],
                    start=(kc == 0), stop=(kc == nkc - 1)))
            pv_prev = tpv
            AE.wait(tpv)
            AE.wait(z_store_tok)
            zevac_prev = AE.opi(lambda e, Q=Q: e.copy(
                z_sb[:, Q * 512:(Q + 1) * 512], PS_Z[:, :]))
            bank_war["Z"] = zevac_prev

        passes = []
        for qt in range(16):
            W = (qt + 1) * 128
            if W <= 1024:
                passes.append((qt, 0, W))
            else:
                passes.append((qt, 0, 1024))
                passes.append((qt, 1024, W))

        for (qt, lo, hi) in passes:
            parq = qt % 4
            smp = sm2[parq]
            mx0, mx1, m01 = smp[:, MX0:MX0+1], smp[:, MX1:MX1+1], smp[:, M01:M01+1]
            nb0, nb1 = smp[:, NB0:NB0+1], smp[:, NB1:NB1+1]
            se0, se1 = smp[:, SE0:SE0+1], smp[:, SE1:SE1+1]
            d0, e0 = smp[:, D0:D0+1], smp[:, E0:E0+1]
            s0e0, ssum = smp[:, S0E0:S0E0+1], smp[:, SSUM:SSUM+1]
            rc, f0 = smp[:, RC:RC+1], smp[:, F0:F0+1]
            pqt = p_sb[parq]
            W = (qt + 1) * 128
            w = hi - lo
            R = pass_idx[0] % 2
            pass_idx[0] += 1
            reg = SC[R]
            PE.wait(region_free[R])
            tsc = None
            for c in range((w + 511) // 512):
                w_c = min(512, w - c * 512)
                tsc = PE.opi(lambda e, c=c, w_c=w_c, qt=qt, lo=lo, reg=reg,
                             qf_=qf_, kf_=kf_: e.matmul(
                    reg[:, c * 512: c * 512 + w_c],
                    qf_[:, qt * 128:(qt + 1) * 128],
                    kf_[:, lo + c * 512: lo + c * 512 + w_c],
                    start=True, stop=True))
            first = (lo == 0)
            last = (hi == W)
            if first and not last:
                VE.wait(tsc)
                VE.wait(chain_war[parq])
                tmx0 = VE.opi(lambda e, reg=reg, mx0=mx0: e.reduce_max(
                    mx0, reg[:, 0:1024], axis=AX.X, negate=True))
                half_pend[parq] = (R, tmx0)
                continue_chain = False
            else:
                continue_chain = True
            if continue_chain:
                VE.wait(tsc)
                if first:
                    VE.wait(chain_war[parq])
                tmk = VE.opi(lambda e, reg=reg, W=W, lo=lo: e.tensor_tensor(
                    reg[:, W - 128 - lo:W - lo], reg[:, W - 128 - lo:W - lo],
                    cmask[:], ALU.add))
                VE.wait(tmk)
                if first and last:
                    tmx = VE.opi(lambda e, reg=reg, w=w, mx0=mx0: e.reduce_max(
                        mx0, reg[:, 0:w], axis=AX.X, negate=True))
                    AE.wait(tmx)
                    AE.wait(p_war[parq])
                    AE.wait(tr_cover["ve"])
                    AE.wait(tr_cover["ae"])
                    texp = AE.opi(lambda e, reg=reg, W=W, pqt=pqt, mx0=mx0,
                                  se0=se0: e.activation(
                        pqt[:, 0:W], reg[:, 0:W], AF.Exp,
                        bias=mx0, scale=1.0, accum_out=se0))
                    region_free[R] = texp
                    VE.wait(texp)
                    trc = VE.opi(lambda e, rc=rc, se0=se0: e.reciprocal(rc, se0))
                    VE.wait(trc)
                    ctok = VE.opi(lambda e, W=W, pqt=pqt, rc=rc:
                                  e.tensor_scalar_mul(
                        pqt[:, 0:W], pqt[:, 0:W], rc))
                    chain_war[parq] = ctok
                    pending_tr.append((qt, ctok))
                else:
                    Ra, tmx0 = half_pend[parq]
                    rega = SC[Ra]
                    tmx1 = VE.opi(lambda e, reg=reg, w=w, mx1=mx1: e.reduce_max(
                        mx1, reg[:, 0:w], axis=AX.X, negate=True))
                    VE.wait(tmx1)
                    tm01 = VE.opi(lambda e, nb1=nb1, mx0=mx0, mx1=mx1:
                                  e.tensor_tensor(nb1, mx0, mx1, ALU.min))
                    AE.wait(tm01)
                    AE.wait(p_war[parq])
                    AE.wait(tr_cover["ve"])
                    AE.wait(tr_cover["ae"])
                    texpa = AE.opi(lambda e, rega=rega, pqt=pqt, nb1=nb1,
                                   se0=se0: e.activation(
                        pqt[:, 0:1024], rega[:, 0:1024], AF.Exp,
                        bias=nb1, scale=1.0, accum_out=se0))
                    region_free[Ra] = texpa
                    texpb = AE.opi(lambda e, reg=reg, w=w, W=W, pqt=pqt,
                                   nb1=nb1, se1=se1: e.activation(
                        pqt[:, 1024:W], reg[:, 0:w], AF.Exp,
                        bias=nb1, scale=1.0, accum_out=se1))
                    region_free[R] = texpb
                    VE.wait(texpb)
                    tsum = VE.opi(lambda e, ssum=ssum, se0=se0, se1=se1:
                                  e.tensor_add(ssum, se0, se1))
                    VE.wait(tsum)
                    trc = VE.opi(lambda e, rc=rc, ssum=ssum: e.reciprocal(
                        rc, ssum))
                    VE.wait(trc)
                    ctok = VE.opi(lambda e, pqt=pqt, W=W, rc=rc:
                                  e.tensor_scalar_mul(
                        pqt[:, 0:W], pqt[:, 0:W], rc))
                    chain_war[parq] = ctok
                    pending_tr.append((qt, ctok))
            if debug and h == 0 and qt == 10 and hi == W:
                SD.wait(chain_war[2])
                ttap = SD.dma(slots["swr"], lambda e: e.dma_start(
                    out=t_pa, in_=p_sb[2][:, :]))
                p_war[2] = ttap
            if fill_i < len(fillers):
                fillers[fill_i]()
                fill_i += 1
            while len(pending_tr) > 0:
                qt2, ct2 = pending_tr.pop(0)
                emit_transposes(qt2, ct2)
                if qt2 % 4 == 3:
                    emit_pv(qt2 // 4)
        while pending_tr:
            qt2, ct2 = pending_tr.pop(0)
            emit_transposes(qt2, ct2)
            if qt2 % 4 == 3:
                emit_pv(qt2 // 4)
        while fill_i < len(fillers):
            fillers[fill_i]()
            fill_i += 1
        if h + 2 < H:
            wload[0] = load_head_weights(h + 2, (wqkv_read[0],))
        SD.wait(zevac_prev)
        z_store_tok = SD.dma(slots["swr"], lambda e, h=h: e.dma_start(
            out=zs[:, :, h, :].rearrange("t p c -> p t c"),
            in_=z_sb.rearrange("p (t c) -> p t c", c=128)))
        if debug and h == 0:
            SD.wait(proj_final[1] if proj_final[1] else z_store_tok)
            SD.dma(slots["swr"], lambda e: e.dma_start(out=t_kf, in_=kset[0][0][:, :]))
            SD.dma(slots["swr"], lambda e: e.dma_start(out=t_qf, in_=kset[0][1][:, :]))
            SD.dma(slots["swr"], lambda e: e.dma_start(out=t_vt, in_=kset[0][2][:, :]))
            SD.wait(chain_war[3])
            SD.dma(slots["swr"], lambda e: e.dma_start(out=t_p, in_=p_sb[3][:, :]))

    barrier()

    ln2_ready0 = [None]
    stg0_bufs = [arE[:, 4096:6144], arE[:, 6144:8192]]
    x1a = [arD[:, 0:2048], arD[:, 2048:4096]]
    ln2_fm = arC[:, 0:8192]
    ln2_3d = ln2_fm.rearrange("p (dc r) -> p dc r", dc=NDC, r=512)
    ln2_read = None

    def emit_ln2_stage(T, bufs=None, lslots=None, store_toks=None):
        nonlocal ln2_read
        bufs = bufs if bufs is not None else x1a
        lslots = lslots if lslots is not None else xsl
        ln2_ready = None
        for tt in range(4):
            t_abs = 4 * T + tt
            par = tt % 2
            war = []
            if bufs is x1a:
                war.append((slots["gwr"], slots["gwr"].cnt))
                if xt_read[par]:
                    war.append(xt_read[par])
            if store_toks is not None:
                war.append(store_toks[t_abs])
            tx = SD.dma(lslots[par], lambda e, t_abs=t_abs, par=par, bufs=bufs:
                        e.dma_start(
                out=bufs[par][:, :], in_=x1s[t_abs * 128:(t_abs + 1) * 128, :]),
                waits=war)
            nmu, rsd, rdy = emit_ln(bufs[par], par, pre_toks=(tx,))
            VE.waits(rdy)
            VE.wait(lntm_read[par])
            ltm = lntm(par)
            tnorm = VE.opi(lambda e, par=par, nmu=nmu, rsd=rsd, bufs=bufs:
                           e.tensor_scalar(
                lntm(par)[:, :], bufs[par][:, :], nmu[:], rsd[:],
                ALU.add, ALU.mult))
            if bufs is x1a:
                xt_read[par] = tnorm
            for g in range(2):
                srcs = [ltm[:, (8 * g + i) * 128:(8 * g + i + 1) * 128]
                        for i in range(8)]
                dst3 = ln2_3d[:, 8 * g:8 * g + 8, tt * 128:(tt + 1) * 128]
                ln2_ready = transpose4(srcs, dst3, src_tok=tnorm,
                                       dst_war=(ln2_read,) if ln2_read else ())
            lntm_read[par] = ln2_ready
        return ln2_ready

    woT = arA
    woT_3d = woT.rearrange("p (fc o) -> p fc o", fc=H, o=D)
    zt = [arB[:, 0:2048], arB[:, 2048:4096]]
    zsl = [slots["z0"], slots["z1"]]
    x1t = arD[:, 4096:6144]
    outb = arD[:, 4096:6144]
    zt_read = [None, None]
    psbig_reader3 = None
    x1_store = None
    x1_stores = {}
    astage = [arE[:, 0:2048], arE[:, 2048:4096]]
    astage_read = [None, None]
    def p3_loads(t):
        par = t % 2
        war = (zt_read[par],) if zt_read[par] else ()
        tz = SD.dma(zsl[par], lambda e, t=t, par=par: e.dma_start(
            out=zt[par].rearrange("p (hh c) -> p hh c", c=128),
            in_=zs[t]), waits=war)
        warx = (xt_read[par],) if xt_read[par] else ()
        tx = SD.dma(xsl[par], lambda e, t=t, par=par: e.dma_start(
            out=xt[par][:, :], in_=x_h[t * 128:(t + 1) * 128, :]), waits=warx)
        return tz, tx

    ld_pend = {0: p3_loads(0)}
    for t in range(NT):
        par = t % 2
        tz, tx = ld_pend.pop(t)
        if t + 1 < NT:
            ld_pend[t + 1] = p3_loads(t + 1)
        PE.wait(tz)
        PE.wait(wo_toks[-1])
        PE.wait(psbig_reader3)
        stg = astage[par]
        tev = None
        for dc in range(4):
            tmm = None
            for fc in range(H):
                tmm = PE.opi(lambda e, dc=dc, fc=fc, par=par: e.matmul(
                    PS_BIG[:, dc * 512:(dc + 1) * 512],
                    zt[par][:, fc * 128:(fc + 1) * 128],
                    woT[:, fc * 2048 + dc * 512: fc * 2048 + (dc + 1) * 512],
                    start=(fc == 0), stop=(fc == H - 1)))
            AE.wait(tmm)
            AE.wait(astage_read[par])
            tev = AE.opi(lambda e, dc=dc, stg=stg: e.copy(
                stg[:, dc * 512:(dc + 1) * 512], PS_BIG[:, dc * 512:(dc + 1) * 512]))
        zt_read[par] = tmm
        psbig_reader3 = tev
        nmu, rsd, rdy = emit_ln(stg, par, pre_toks=(tev,))
        VE.waits(rdy)
        VE.wait(x1_store)
        tts = VE.opi(lambda e, nmu=nmu, rsd=rsd, stg=stg: e.tensor_scalar(
            x1t[:, :], stg[:, :], nmu[:], rsd[:], ALU.add, ALU.mult))
        astage_read[par] = tts
        VE.wait(tx)
        VE.wait(tts)
        tadd = VE.opi(lambda e, par=par: e.tensor_add(
            x1t[:, :], x1t[:, :], xt[par][:, :]))
        xt_read[par] = tadd
        GD.wait(tadd)
        x1_store = GD.dma(slots["gwr"], lambda e, t=t: e.dma_start(
            out=x1s[t * 128:(t + 1) * 128, :], in_=x1t[:, :]))
        x1_stores[t] = x1_store

    barrier()

    h_sb = arA
    h_3d = h_sb.rearrange("p (it r) -> p it r", it=NIT, r=512)
    wslab = [arB[:, i * 4096:(i + 1) * 4096] for i in range(4)]
    wslab_slots = [slots["wg0"], slots["wg1"], slots["wu0"], slots["wu1"]]
    wx = arC[:, 9216:17408]
    wx_g, wx_u = wx[:, 0:4096], wx[:, 4096:8192]
    sg = [arC[:, 8192:8704], arC[:, 8704:9216]]
    y_tm = arE[:, 0:8192]
    y_3d = y_tm.rearrange("p (s r) -> p s r", s=4, r=2048)
    y_fm = [arE[:, 8192:8704], arE[:, 8704:9216]]

    wslab_read = [None, None, None, None]
    wx_read = None
    wx_toks = [None, None]
    wd0_tok = None
    sg_war = [None, None]
    h_read = None
    yfm_read = [None, None]
    ytm_read = None
    out_store = None
    outb_war = None
    IPS = ISLAB // 128

    def load_wx_sl0(war):
        t1 = GD.dma(slots["wx0"], lambda e: e.dma_start(
            out=wx_g.rearrange("p (dc i) -> p dc i", i=ISLAB), in_=wg_h[0]),
            waits=war)
        t2 = GD.dma(slots["wx1"], lambda e: e.dma_start(
            out=wx_u.rearrange("p (dc i) -> p dc i", i=ISLAB), in_=wu_h[0]))
        return [t1, t2]

    wx_toks = load_wx_sl0(())
    ln2_ready = emit_ln2_stage(0)
    for T in range(NTS):
        if debug and T == 0:
            SD.wait(ln2_ready)
            SD.dma(slots["swr"], lambda e: e.dma_start(out=t_ln2, in_=ln2_fm[:, :]))
        PE.wait(ln2_ready)
        PE.wait(t_ones)
        PE.wait(bank_war["A"])
        tls = None
        for dc in range(NDC):
            tls = PE.opi(lambda e, dc=dc: e.matmul(
                PS_A[0:1, :], ones16[:, 0:1], ln2_fm[:, dc * 512:(dc + 1) * 512],
                start=(dc == 0), stop=(dc == NDC - 1)))
        AE.wait(tls)
        tnl = AE.opi(lambda e: e.mul(negln[:], PS_A[0:1, :], -128.0))
        bank_war["A"] = tnl

        def load_wslab(m, sl):
            buf = 2 * m + (sl % 2)
            war = (wslab_read[buf],) if wslab_read[buf] else ()
            src = wg_h if m == 0 else wu_h
            return GD.dma(wslab_slots[buf], lambda e, sl=sl, buf=buf, src=src:
                          e.dma_start(out=wslab[buf].rearrange(
                              "p (dc i) -> p dc i", i=ISLAB), in_=src[sl]),
                          waits=war)

        pend = {(0, 1): load_wslab(0, 1), (1, 1): load_wslab(1, 1)}
        hdone = None
        for sl in range(NSLAB):
            if sl == 0:
                gtok, utok = wx_toks
                gsrc, usrc = wx_g, wx_u
            else:
                gtok = pend.pop((0, sl))
                utok = pend.pop((1, sl))
                gsrc = wslab[0 + (sl % 2)]
                usrc = wslab[2 + (sl % 2)]
            if sl + 1 < NSLAB:
                pend[(0, sl + 1)] = load_wslab(0, sl + 1)
                pend[(1, sl + 1)] = load_wslab(1, sl + 1)
            if sl == 1:
                war = [wx_read] if wx_read else []
                wd0_tok = GD.dma(slots["wx0"], lambda e, T=T: e.dma_start(
                    out=wx.rearrange("p (ic r) -> p ic r", r=128), in_=wd_h[0]),
                    waits=war)
            for ir in range(IPS):
                it = sl * IPS + ir
                par = it % 2
                PE.wait(gtok)
                PE.wait(bank_war["A"])
                for dc in range(NDC):
                    PE.opi(lambda e, gsrc=gsrc, dc=dc, ir=ir: e.matmul(
                        PS_A[:, :],
                        gsrc[:, dc * ISLAB + ir * 128: dc * ISLAB + (ir + 1) * 128],
                        ln2_fm[:, dc * 512:(dc + 1) * 512], start=(dc == 0), stop=False))
                gmm = PE.opi(lambda e: e.matmul(
                    PS_A[:, :], ones16[0:1, 0:128], negln[:],
                    start=False, stop=True, skip_group_check=True))
                PE.wait(utok)
                PE.wait(bank_war["Z"])
                for dc in range(NDC):
                    PE.opi(lambda e, usrc=usrc, dc=dc, ir=ir: e.matmul(
                        PS_Z[:, :],
                        usrc[:, dc * ISLAB + ir * 128: dc * ISLAB + (ir + 1) * 128],
                        ln2_fm[:, dc * 512:(dc + 1) * 512], start=(dc == 0), stop=False))
                umm = PE.opi(lambda e: e.matmul(
                    PS_Z[:, :], ones16[0:1, 0:128], negln[:],
                    start=False, stop=True, skip_group_check=True))
                ln2_read = umm
                if sl == 0:
                    wx_read = umm
                else:
                    wslab_read[0 + (sl % 2)] = gmm
                    wslab_read[2 + (sl % 2)] = umm
                AE.wait(gmm)
                AE.wait(sg_war[par])
                tsi = AE.opi(lambda e, it=it, par=par: e.activation(
                    sg[par][:, :], PS_A[:, :], AF.Silu, scale=gsc[:, it:it + 1]))
                bank_war["A"] = tsi
                VE.wait(umm)
                VE.wait(tsi)
                VE.wait(h_read)
                th = VE.opi(lambda e, it=it, par=par: e.scalar_tensor_tensor(
                    h_sb[:, it * 512:(it + 1) * 512], PS_Z[:, :], usc[:, it:it + 1],
                    sg[par][:, :], ALU.mult, ALU.mult))
                bank_war["Z"] = th
                sg_war[par] = th
                hdone = th
        h_read = None
        if debug and T == 0:
            SD.wait(hdone)
            SD.dma(slots["swr"], lambda e: e.dma_start(out=t_h, in_=h_sb[:, :]))
        PE.wait(hdone)
        PE.wait(bank_war["A"])
        ths = None
        for ic in range(NIT):
            ths = PE.opi(lambda e, ic=ic: e.matmul(
                PS_A[0:1, :], ones16[:, 0:1], h_sb[:, ic * 512:(ic + 1) * 512],
                start=(ic == 0), stop=(ic == NIT - 1)))
        AE.wait(ths)
        tnh = AE.opi(lambda e: e.mul(neghs[:], PS_A[0:1, :], -128.0))
        bank_war["A"] = tnh

        wd_read = [None, None]

        def load_wd(dt):
            par = dt % 2
            b0 = 0 if par == 0 else 2
            war = []
            if wd_read[par] is not None:
                war.append(wd_read[par])
            if dt < 3:
                for b in (b0, b0 + 1):
                    if wslab_read[b] is not None:
                        war.append(wslab_read[b])
            wdv = arB[:, b0 * 4096: b0 * 4096 + 8192]
            tok = GD.dma(wslab_slots[b0], lambda e, dt=dt, wdv=wdv: e.dma_start(
                out=wdv.rearrange("p (ic r) -> p ic r", r=128), in_=wd_h[dt]),
                waits=war)
            return tok

        wd_pend = {1: load_wd(1)}
        last_yt = None
        for dt in range(NDC):
            if dt == 0:
                wtok = wd0_tok
                wdv = wx
            else:
                wtok = wd_pend.pop(dt)
                b0 = 0 if dt % 2 == 0 else 2
                wdv = arB[:, b0 * 4096: b0 * 4096 + 8192]
            if dt + 1 < NDC:
                wd_pend[dt + 1] = load_wd(dt + 1)
            par = dt % 2
            bk = "A" if par == 0 else "Z"
            if dt == 1:
                if T + 1 < NTS:
                    nonlocal_war = [h_read] if h_read else []
                    wx_toks[:] = [
                        GD.dma(slots["wx0"], lambda e: e.dma_start(
                            out=wx_g.rearrange("p (dc i) -> p dc i", i=ISLAB),
                            in_=wg_h[0]), waits=nonlocal_war),
                        GD.dma(slots["wx1"], lambda e: e.dma_start(
                            out=wx_u.rearrange("p (dc i) -> p dc i", i=ISLAB),
                            in_=wu_h[0]))]
            PE.wait(wtok)
            PE.wait(t_ones32)
            PE.wait(bank_war[bk])
            for ic in range(NIT):
                PE.opi(lambda e, bk=bk, wdv=wdv, ic=ic: e.matmul(
                    projb[bk][:, :], wdv[:, ic * 128:(ic + 1) * 128],
                    h_sb[:, ic * 512:(ic + 1) * 512], start=(ic == 0), stop=False))
            tdm = PE.opi(lambda e, bk=bk: e.matmul(
                projb[bk][:, :], ones32r[:], neghs[:],
                start=False, stop=True, skip_group_check=True))
            h_read = tdm
            wd_read[par] = tdm
            if dt == 0:
                wx_read = tdm
            else:
                wslab_read[2 * par] = tdm
                wslab_read[2 * par + 1] = tdm
            AE.wait(tdm)
            AE.wait(yfm_read[par])
            tye = AE.opi(lambda e, bk=bk, dt=dt, par=par: e.mul(
                y_fm[par][:, :], projb[bk][:, :], dsc[:, dt:dt + 1]))
            bank_war[bk] = tye
            srcs = [y_fm[par][:, i * 128:(i + 1) * 128] for i in range(4)]
            dst3 = y_3d[:, :, dt * 128:(dt + 1) * 128]
            last_yt = transpose4(srcs, dst3, src_tok=tye,
                                 dst_war=(ytm_read,) if ytm_read else (), f32=True)
            yfm_read[par] = last_yt
        if debug and T == 0:
            SD.wait(last_yt)
            SD.dma(slots["swr"], lambda e: e.dma_start(out=t_y, in_=y_tm[:, 0:2048]))
        if T + 1 < NTS:
            ln2_ready = emit_ln2_stage(T + 1)
        for tt in range(4):
            t_abs = 4 * T + tt
            par = tt % 2
            warx = [(slots["gwr"], slots["gwr"].cnt)]
            if xt_read[par]:
                warx.append(xt_read[par])
            tx = SD.dma(xsl[par], lambda e, t_abs=t_abs, par=par: e.dma_start(
                out=x1a[par][:, :], in_=x1s[t_abs * 128:(t_abs + 1) * 128, :]),
                waits=warx)
            ytile = y_tm[:, tt * 2048:(tt + 1) * 2048]
            nmu, rsd, rdy = emit_ln(ytile, par, pre_toks=(last_yt,))
            VE.waits(rdy)
            VE.wait(outb_war)
            tts = VE.opi(lambda e, ytile=ytile, nmu=nmu, rsd=rsd: e.tensor_scalar(
                outb[:, :], ytile, nmu[:], rsd[:], ALU.add, ALU.mult))
            ytm_read = tts
            VE.wait(tx)
            VE.wait(tts)
            tadd = VE.opi(lambda e, par=par: e.tensor_add(
                x1a[par][:, :], x1a[par][:, :], outb[:, :]))
            outb_war = tadd
            GD.wait(tadd)
            out_store = GD.dma(slots["gwr"], lambda e, t_abs=t_abs, par=par:
                               e.dma_start(
                out=out_h[t_abs * 128:(t_abs + 1) * 128, :], in_=x1a[par][:, :]))
            xt_read[par] = out_store

    GD.wait(out_store)
    barrier()

    with nc.Block() as block:
        def mk(stream):
            def runner(eng):
                for item in stream.ops:
                    k = item[0]
                    if k == "w":
                        eng.wait_ge(item[1].sem, item[2])
                    elif k == "o":
                        item[1](eng)
                    elif k == "i":
                        item[1](eng).then_inc(stream.sem, stream.unit)
                    else:
                        item[1](eng).then_inc(item[2].sem, 16)
            return runner
        block.tensor(mk(PE))
        block.vector(mk(VE))
        block.scalar(mk(AE))
        block.gpsimd(mk(GD))
        block.sync(mk(SD))

    es.close()
    return nc


def _prep(L0_attn_x, L0_attn_W_K, L0_attn_W_Q, L0_attn_W_V, L0_attn_W_O,
          L0_ffn_gate_w, L0_ffn_gate_scale, L0_ffn_up_w, L0_ffn_up_scale,
          L0_ffn_down_w, L0_ffn_down_scale):
    c = np.ascontiguousarray
    f32 = np.float32
    x = np.asarray(L0_attn_x, dtype=f32)

    def wfmt(w):
        w = np.asarray(w, dtype=f32)
        return c(w.transpose(0, 2, 1).reshape(H, NDC, 128, DH).transpose(0, 2, 1, 3))

    wqH = wfmt(L0_attn_W_Q)
    wkH = wfmt(L0_attn_W_K)
    wvH = wfmt(L0_attn_W_V)
    woH = c(np.asarray(L0_attn_W_O, dtype=f32).T.reshape(NDC, 128, D).transpose(1, 0, 2))

    ffn_u8 = all(int(np.asarray(w).min()) >= 0 and int(np.asarray(w).max()) <= 255
                 for w in (L0_ffn_gate_w, L0_ffn_up_w, L0_ffn_down_w))
    qdt = np.uint8 if ffn_u8 else np.int32

    def gfmt(w):
        w = np.asarray(w, dtype=np.int32).astype(qdt)
        return c(w.reshape(NSLAB, ISLAB, NDC, 128).transpose(0, 3, 2, 1))

    wgH = gfmt(L0_ffn_gate_w)
    wuH = gfmt(L0_ffn_up_w)
    wdH = c(np.asarray(L0_ffn_down_w, dtype=np.int32).astype(qdt)
            .reshape(NDC, 128, NIT, 128).transpose(0, 3, 2, 1))
    gscH = c(np.asarray(L0_ffn_gate_scale, dtype=f32).reshape(NIT, 128).T)
    uscH = c(np.asarray(L0_ffn_up_scale, dtype=f32).reshape(NIT, 128).T)
    dscH = c(np.asarray(L0_ffn_down_scale, dtype=f32).reshape(NDC, 128).T)
    id16H = np.eye(128, dtype=bfdt)
    id32H = np.eye(128, dtype=f32)
    r = np.arange(128)
    cmaskH = np.where(r[:, None] >= r[None, :], 0.0, NEG_BIG).astype(f32)
    shared = dict(wqH=wqH, wkH=wkH, wvH=wvH, woH=woH, wgH=wgH, wuH=wuH, wdH=wdH,
                  gscH=gscH, uscH=uscH, dscH=dscH, id16H=id16H, id32H=id32H,
                  cmaskH=cmaskH)
    return x, shared, ffn_u8


def kernel(**inputs):
    x, shared, ffn_u8 = _prep(**inputs)
    nc = build(debug=DEBUG, ffn_u8=ffn_u8)
    in_maps = [dict(shared, x=np.ascontiguousarray(x[cid])) for cid in range(8)]
    res = run_bass_kernel_spmd(nc, in_maps, core_ids=list(range(8)))
    return np.stack([res.results[cid]["out"] for cid in range(8)], axis=0)
